# version 08
# baseline (speedup 1.0000x reference)
import contextlib
import numpy as np
import ml_dtypes
import concourse.bass as bass
import concourse.mybir as mybir
from concourse.bass_utils import run_bass_kernel_spmd

F32 = mybir.dt.float32
BF16 = mybir.dt.bfloat16
AF = mybir.ActivationFunctionType
ALU = mybir.AluOpType
AX = mybir.AxisListType

NCORES = 8
B, T, TSH, C = 8, 4096, 512, 1024
H, KV, QL = 16, 64, 512
NKT, NTT, NST, NHP = 8, 4, 32, 8
NG = 16
BETA = 0.55
LOG_BETA8 = float(np.log(BETA / 8.0))


class Rec:
    def __init__(self):
        self.ops = []
        self.cnt = {}

    def op(self, eng, fn, waits=(), inc=None, inc_n=1):
        incs = []
        if inc is not None:
            self.cnt[inc] = self.cnt.get(inc, 0) + inc_n
            incs = [(inc, inc_n)]
        self.ops.append((eng, fn, [w for w in waits if w is not None and w[1] > 0],
                         incs))
        return self.cnt.get(inc)

    def c(self, sem):
        return self.cnt.get(sem, 0)


DEBUG_BUILD = False
STOP_STAGE = None
TAPS = False
NO_INTERLEAVE = False
P0_SKIP = set()


class _StopBuild(Exception):
    pass


def build_program():
    nc = bass.Bass("TRN2", target_bir_lowering=False, debug=DEBUG_BUILD)

    dxf = nc.dram_tensor("xf", [B, C, T], F32, kind="ExternalInput").ap()
    dxo = nc.dram_tensor("xo", [B, C, TSH], F32, kind="ExternalInput").ap()
    dwqdT = nc.dram_tensor("wqdT", [C, QL], F32, kind="ExternalInput").ap()
    dwqkT = nc.dram_tensor("wqkT", [QL, C], F32, kind="ExternalInput").ap()
    dwkvT = nc.dram_tensor("wkvT", [C, KV], F32, kind="ExternalInput").ap()
    dwvuT = nc.dram_tensor("wvuT", [KV, C], F32, kind="ExternalInput").ap()
    dwvu = nc.dram_tensor("wvu", [C, KV], F32, kind="ExternalInput").ap()
    dwprT = nc.dram_tensor("wprT", [C, C], F32, kind="ExternalInput").ap()
    dW = {nm: nc.dram_tensor(nm, [C, C], F32, kind="ExternalInput").ap()
          for nm in ["wfT", "ufT", "wiT", "uiT", "wcT", "ucT", "woT", "uoT"]}
    drow = {nm: nc.dram_tensor(nm, [dim], F32, kind="ExternalInput").ap()
            for nm, dim in [("bqd", QL), ("bqk", C), ("bkv", KV), ("bvu", C),
                            ("bpr", C), ("bfb", C), ("bib", C), ("bcb", C),
                            ("bob", C), ("incb", C)]}
    dy = nc.dram_tensor("y", [B, C, TSH], F32, kind="ExternalOutput").ap()
    taps = {}
    if TAPS:
        for nm, sh in [("t_wcomb", [128, NKT * C]), ("t_colf", [128, 48]),
                       ("t_colf2", [128, 24]), ("t_negm", [128, NHP]),
                       ("t_maxk", [1, 8]), ("t_q", [128, NKT * TSH]),
                       ("t_ln1", [128, NKT * TSH]), ("t_ycn", [128, NKT * TSH]),
                       ("t_ckvf", [64, T]), ("t_x1", [C, TSH]),
                       ("t_ln2", [C, TSH]), ("t_c", [128, NKT * TSH]),
                       ("t_wbar", [128, KV]), ("t_wvu", [128, 512]), ("t_wpr0", [128, 1024]), ("t_colb", [128, 28]), ("t_gates", [384, 512])]:
            taps[nm] = nc.dram_tensor(nm, sh, F32, kind="ExternalOutput").ap()

    d_ckv_rm = nc.dram_tensor("d_ckv_rm", [B, NST, 128, KV], BF16).ap()
    d_ckv_fm = nc.dram_tensor("d_ckv_fm", [B, KV, T], BF16).ap()
    d_ln2 = nc.dram_tensor("d_ln2", [B, C, TSH], BF16).ap()
    d_x1 = nc.dram_tensor("d_x1", [B, C, TSH], F32).ap()
    d_zco = nc.dram_tensor("d_zco", [C, TSH], F32).ap()

    st = contextlib.ExitStack()
    with st:
        SB = lambda nm, sh, dt: st.enter_context(nc.sbuf_tensor(nm, sh, dt))
        PSA = lambda nm, sh: st.enter_context(nc.psum_tensor(nm, sh, F32))

        identf = SB("identf", [1, 1], F32)
        ident128 = SB("ident128", [128, 128], F32)
        ones_col = SB("ones_col", [128, 1], BF16)
        ones_row = SB("ones_row", [1, 128], BF16)
        eps_row = SB("eps_row", [1, 1], F32)
        eps_col = SB("eps_col", [128, 1], F32)
        coef_row = SB("coef_row", [1, 1], F32)
        vflush = SB("vflush", [1, 1], F32)

        wqdT_sb = SB("wqdT_sb", [128, NKT, QL], BF16)
        wqkT_sb = SB("wqkT_sb", [128, 4, C], BF16)
        wkv1_sb = SB("wkv1_sb", [128, NKT, KV + 2], BF16)
        WcombT_sb = SB("WcombT_sb", [128, NKT, C], BF16)
        strm = SB("strm", [128, 12288], BF16)

        colf = SB("colf", [128, 48], F32)
        colf2 = SB("colf2", [128, 24], F32)
        colb = SB("colb", [128, 28], BF16)
        wbar_b = SB("wbar_b", [128, KV], F32)
        rows = SB("rows", [2, 1536], F32)
        rows2 = SB("rows2", [2, 1280], F32)
        rows_bf = SB("rows_bf", [1, 1024], BF16)
        tiny = SB("tiny", [1, 8], F32)
        maxk2_sb = SB("maxk2_sb", [1, 8], F32)
        negM_col = SB("negM_col", [128, NHP], F32)
        knorm_sb = SB("knorm_sb", [128, NST], F32)
        rcol = SB("rcol", [128, 8], F32)

        xb_sb = SB("xb_sb", [128, NKT, TSH], BF16)
        ln1_sb = SB("ln1_sb", [128, NKT, TSH], BF16)
        cq_sb = SB("cq_sb", [128, 4, TSH], BF16)
        q_sb = SB("q_sb", [128, NKT, TSH], BF16)
        ln2_sb = SB("ln2_sb", [128, 2, NKT, TSH], BF16)
        ckvf_sb = SB("ckvf_sb", [128, T], BF16)
        ckvr_sb = SB("ckvr_sb", [128, NST, KV + 2], BF16)
        attn_sb = SB("attn_sb", [128, 2, 2048], BF16)
        ycn_sb = SB("ycn_sb", [128, NKT, TSH], BF16)
        c_sb = SB("c_sb", [128, NKT, TSH], F32)
        cbf_sb = SB("cbf_sb", [128, 2, NKT, TSH], BF16)
        fgate = SB("fgate", [128, TSH], F32)
        igate = SB("igate", [128, TSH], F32)
        ggate = SB("ggate", [128, TSH], F32)
        tfc = SB("tfc", [128, TSH], F32)
        tig = SB("tig", [128, TSH], F32)
        pool = SB("pool", [128, 8, 512], F32)
        poolb = SB("poolb", [128, 2, 512], BF16)
        xq_sb = SB("xq_sb", [128, 1, 1024], BF16)

        ps_sc = PSA("ps_sc", [128, 2048])
        ps_ycA = PSA("ps_ycA", [128, 512])
        ps_ycB = PSA("ps_ycB", [128, 512])
        ps_w1 = PSA("ps_w1", [128, 512])
        ps_w2 = PSA("ps_w2", [128, 512])

        sems = {}
        for nm in ["pe", "ve", "ac", "gp", "dW", "dXFa", "dXFb", "dXO", "dXB",
                   "dCK", "dCS", "dSP", "dFS", "dRB", "dZC", "dOU"]:
            sems[nm] = st.enter_context(nc.semaphore(nm))

        R = Rec()

        def emit_taps():
            if TAPS:
                vop(lambda e: e.tensor_copy(pool[:, 0, 0:48], colf[:, :]),
                    waits=[("dOU", R.c("dOU"))])
                vfl()
                sdma(taps["t_colf"], pool[:, 0, 0:48], "dOU", waits=[("ve", R.c("ve"))])
                vop(lambda e: e.tensor_copy(pool[:, 0, 48:72], colf2[:, :]))
                vop(lambda e: e.tensor_copy(pool[:, 1, 100:128], colb[:, :]))
                vfl()
                sdma(taps["t_colf2"], pool[:, 0, 48:72], "dOU", waits=[("ve", R.c("ve"))])
                sdma(taps["t_colb"], pool[:, 1, 100:128], "dOU")
                vop(lambda e: e.tensor_copy(pool[:, 0, 72:80], negM_col[:, :]))
                vop(lambda e: e.tensor_copy(pool[:, 1, 0:KV], wbar_b[:, :]))
                vfl()
                sdma(taps["t_negm"], pool[:, 0, 72:80], "dOU", waits=[("ve", R.c("ve"))])
                sdma(taps["t_wbar"], pool[:, 1, 0:KV], "dOU")
                sdma(taps["t_maxk"], maxk2_sb[:, :], "dOU")
                gdma(taps["t_wcomb"],
                     WcombT_sb[:, :, :].rearrange("p k t -> p (k t)"), "dOU",
                     waits=[("dOU", R.c("dOU")), ("ve", R.c("ve")), ("pe", R.c("pe")), ("ac", R.c("ac")), ("gp", R.c("gp")), ("dSP", R.c("dSP")), ("dCS", R.c("dCS"))])
                for nm, buf in [("t_q", q_sb), ("t_ln1", ln1_sb), ("t_ycn", ycn_sb)]:
                    for kt in range(NKT):
                        vop(lambda e, kt=kt, buf=buf: e.tensor_copy(pool[:, kt % 2, :],
                                                                    buf[:, kt, :]),
                            waits=[("dOU", R.c("dOU"))])
                        vfl()
                        sdma(taps[nm].rearrange("p (k t) -> p k t", k=NKT)[:, kt, :],
                             pool[:, kt % 2, :], "dOU", waits=[("ve", R.c("ve"))])
                for kt in range(NKT):
                    vop(lambda e, kt=kt: e.tensor_copy(pool[:, kt % 2, :], c_sb[:, kt, :]),
                        waits=[("dOU", R.c("dOU"))])
                    vfl()
                    sdma(taps["t_c"].rearrange("p (k t) -> p k t", k=NKT)[:, kt, :],
                         pool[:, kt % 2, :], "dOU", waits=[("ve", R.c("ve"))])
                gdma(taps["t_ckvf"], d_ckv_fm[0], "dOU", waits=[("dOU", R.c("dOU")), ("ve", R.c("ve")), ("pe", R.c("pe")), ("ac", R.c("ac")), ("gp", R.c("gp")), ("dSP", R.c("dSP")), ("dCS", R.c("dCS"))])
                vop(lambda e: e.tensor_copy(pool[:, 0, :], strm[:, 8192:8704]),
                    waits=[("dOU", R.c("dOU"))])
                vop(lambda e: e.tensor_copy(pool[:, 2, :], strm[:, 0:512]))
                vop(lambda e: e.tensor_copy(pool[:, 3, :], strm[:, 512:1024]))
                vfl()
                sdma(taps["t_wvu"], pool[:, 0, :], "dOU", waits=[("ve", R.c("ve"))])
                sdma(taps["t_wpr0"][:, 0:512], pool[:, 2, :], "dOU")
                sdma(taps["t_wpr0"][:, 512:1024], pool[:, 3, :], "dOU")
                sdma(taps["t_x1"], d_x1[0], "dOU")
                sdma(taps["t_gates"], d_zco[0:384, :], "dOU",
                     waits=[("dSP", R.c("dSP"))])
                gdma(taps["t_ln2"], d_ln2[0], "dOU", waits=[("dOU", R.c("dOU")), ("ve", R.c("ve")), ("pe", R.c("pe")), ("ac", R.c("ac")), ("gp", R.c("gp")), ("dSP", R.c("dSP")), ("dCS", R.c("dCS"))])


        def do_replay():
            emit_taps()
            R.op("sync", lambda e: None, waits=[("dOU", R.c("dOU"))])
            by_eng = {"gpsimd": [], "vector": [], "tensor": [], "scalar": [],
                      "sync": []}
            for eng, fn, waits, incs in R.ops:
                by_eng[eng].append((fn, waits, incs))

            def run_section(eng_obj, ops, drain_each=False):
                prev_was_real = False
                for fn, waits, incs in ops:
                    for s, v in waits:
                        eng_obj.wait_ge(sems[s], v)
                    if drain_each and prev_was_real:
                        eng_obj.drain()
                    inst = fn(eng_obj)
                    prev_was_real = inst is not None
                    for s, n in incs:
                        inst = inst.then_inc(sems[s], n)

            block = st.enter_context(nc.Block())

            @block.gpsimd
            def _(e):
                run_section(e, by_eng["gpsimd"])

            @block.vector
            def _(e):
                run_section(e, by_eng["vector"], drain_each=True)

            @block.tensor
            def _(e):
                run_section(e, by_eng["tensor"])

            @block.scalar
            def _(e):
                run_section(e, by_eng["scalar"])

            @block.sync
            def _(e):
                run_section(e, by_eng["sync"])
            return nc

        def gdma(dst, src, sem, waits=()):
            return R.op("gpsimd", lambda e, d=dst, s=src: e.dma_start(out=d, in_=s),
                        waits=waits, inc=sem, inc_n=16)

        def sdma(dst, src, sem, waits=(), slow=False):
            return R.op("sync", lambda e, d=dst, s=src, sl=slow: e.dma_start(
                out=d, in_=s, allow_slow_non_contiguous=sl),
                        waits=waits, inc=sem, inc_n=16)

        def vop(fn, waits=(), inc=None):
            return R.op("vector", fn, waits=waits, inc=inc)

        def vfl(waits=()):
            return R.op("vector", lambda e: e.drain(), waits=waits, inc="ve")

        def sop(fn, waits=(), inc=None, drain=True):
            if inc is not None and drain:
                R.op("scalar", fn, waits=waits)
                return R.op("scalar", lambda e: e.drain(), inc=inc)
            return R.op("scalar", fn, waits=waits, inc=inc)

        def top(fn, waits=(), inc=None):
            return R.op("tensor", fn, waits=waits, inc=inc)

        def tmark():
            for i in range(len(R.ops) - 1, -1, -1):
                eng, fn, waits, incs = R.ops[i]
                if eng == "tensor":
                    R.cnt["pe"] = R.cnt.get("pe", 0) + 1
                    incs.append(("pe", 1))
                    return R.op("tensor", lambda e: e.drain(), inc="pe")
            raise RuntimeError("tmark with no prior tensor op")

        def tfl(waits=()):
            return R.op("tensor", lambda e: e.drain(), waits=waits, inc="pe")

        R.op("gpsimd", lambda e: e.memset(ident128[:, :], 0.0), inc="gp")
        gpm = R.c("gp")
        R.op("gpsimd", lambda e: e.affine_select(
            out=ident128[:, :], in_=ident128[:, :], compare_op=ALU.not_equal,
            fill=1.0, base=0, pattern=[[-1, 128]], channel_multiplier=1),
            waits=[("gp", gpm)], inc="gp")
        for fn in [lambda e: e.memset(identf[:, :], 1.0),
                   lambda e: e.memset(ones_col[:, :], 1.0),
                   lambda e: e.memset(ones_row[:, :], 1.0),
                   lambda e: e.memset(eps_row[:, :], 1e-5),
                   lambda e: e.memset(eps_col[:, :], 1e-5),
                   lambda e: e.memset(rows[0:2, :], 0.0),
                   lambda e: e.memset(rows2[0:2, :], 0.0),
                   lambda e: e.memset(coef_row[:, :], LOG_BETA8),
                   lambda e: e.memset(wkv1_sb[:, :, :], 1.0),
                   lambda e: e.memset(ckvr_sb[:, :, :], 1.0)]:
            R.op("gpsimd", fn, inc="gp")
        R.op("gpsimd", lambda e: e.memset(vflush[:, :], 0.0), inc="gp")
        GP0 = R.c("gp")

        gdma(wqdT_sb[:, :, :], dwqdT.rearrange("(k p) q -> p k q", p=128), "dW")
        gdma(wqkT_sb[:, :, :], dwqkT.rearrange("(k p) o -> p k o", p=128), "dW")
        gdma(wkv1_sb[:, :, 0:KV], dwkvT.rearrange("(k p) v -> p k v", p=128), "dW", waits=[("gp", GP0)])
        gdma(strm[0:KV, 8704:9728], dwvuT, "dW")
        wprojT_v = strm[:, 0:8192].rearrange("p (k o) -> p k o", k=NKT)
        gdma(wprojT_v, dwprT.rearrange("(k p) o -> p k o", p=128), "dW")
        wvu_v = strm[:, 8192:8704].rearrange("p (k v) -> p k v", k=NKT)
        gdma(wvu_v, dwvu.rearrange("(k p) v -> p k v", p=128), "dW")
        DW_ALL = R.c("dW")

        def bias_cols(src_row, n, dstcol, dstoff, to_bf=None, bfoff=0):
            nblk = (n + 127) // 128
            np_ = min(128, n)
            sdma(dstcol[:, dstoff:dstoff + nblk][0:np_, :],
                 src_row.rearrange("(k p) -> p k", p=np_), "dRB",
                 waits=[("ve", R.c("ve"))], slow=True)
            drb = R.c("dRB")
            vop(lambda e: e.tensor_copy(vflush[:, :], vflush[:, :]),
                waits=[("dRB", drb), ("gp", GP0)])
            if to_bf is not None:
                vop(lambda e, nb=nblk, dc=dstcol, do=dstoff, tb=to_bf, bo=bfoff,
                    np_=np_: e.tensor_copy(
                        tb[:, bo:bo + 2 * nb].rearrange("p (k two) -> p k two", two=2)[0:np_, :, 0:1],
                        dc[:, do:do + nb][0:np_, :]
                        .rearrange("p (k one) -> p k one", one=1)))
            vfl()
            return R.c("ve")

        if "bias" not in P0_SKIP:
            bias_cols(drow["bfb"], C, colf, 16)
            bias_cols(drow["bib"], C, colf, 24)
            bias_cols(drow["bcb"], C, colf, 32)
            bias_cols(drow["bob"], C, colf2, 0)
            bias_cols(drow["incb"], C, colf2, 8)
        bias_cols(drow["bqd"], QL, colf2, 16, to_bf=colb, bfoff=16)
        VE_BKV = bias_cols(drow["bkv"], KV, colf2, 20, to_bf=colb, bfoff=24)

        if "wbar" not in P0_SKIP:
            twq = [("dW", DW_ALL), ("gp", GP0)]
            for kt in range(NKT):
                top(lambda e, kt=kt: e.matmul(ps_w2[0:KV, 0:1], wkv1_sb[:, kt, 0:KV],
                                              ones_col[:, :], start=(kt == 0),
                                              stop=(kt == NKT - 1), skip_group_check=True),
                    waits=twq if kt == 0 else [])
            tmark()
            vop(lambda e: e.tensor_copy(colf2[0:KV, 21:22], ps_w2[0:KV, 0:1]),
                waits=[("pe", R.c("pe"))])
            vfl()
            top(lambda e: e.transpose(ps_w1[0:1, 0:KV], colf2[0:KV, 21:22],
                                      ident128[0:KV, 0:KV]), waits=[("ve", R.c("ve"))])
            tmark()
            tfl()
            vop(lambda e: e.tensor_copy(rows[0:1, 1024:1024 + KV], ps_w1[0:1, 0:KV]),
                waits=[("pe", R.c("pe"))])
            vop(lambda e: e.tensor_copy(rows_bf[:, 0:KV], rows[0:1, 1024:1024 + KV]))
            vfl()
            top(lambda e: e.matmul(ps_w1[:, 0:KV], ones_row[0:1, :], rows_bf[0:1, 0:KV],
                                   start=True, stop=True, skip_group_check=True), waits=[("ve", R.c("ve"))])
            tmark()
            vop(lambda e: e.tensor_copy(wbar_b[:, :], ps_w1[:, 0:KV]),
                waits=[("pe", R.c("pe"))])
            vfl()

        if "wcomb" not in P0_SKIP:
            for kt in range(NKT):
                for base in (0, 64):
                    for ch in range(2):
                        top(lambda e, kt=kt, base=base, ch=ch: e.matmul(
                            ps_w2[0:64, 0:512],
                            wvu_v[:, kt, :][base:base + 64, :],
                            wprojT_v[:, kt, ch * 512:(ch + 1) * 512][base:base + 64, :],
                            start=True, stop=True, tile_position=(base, 0), skip_group_check=True),
                            waits=[("ve", R.c("ve")), ("dW", DW_ALL)])
                        tmark()
                        vop(lambda e, kt=kt, base=base, ch=ch: e.tensor_copy(
                            WcombT_sb[:, kt, ch * 512:(ch + 1) * 512][base:base + 64, :],
                            ps_w2[0:64, 0:512]),
                            waits=[("pe", R.c("pe"))])
                        vfl()

        sdma(colf[:, 40:48], drow["bvu"].rearrange("(k p) -> p k", p=128), "dRB",
             waits=[("ve", R.c("ve"))], slow=True)
        DRB_BVU = R.c("dRB")
        for ot in range(NKT):
            top(lambda e, ot=ot: e.matmul(
                ps_w1[:, ot:ot + 1], strm[0:KV, 8704 + ot * 128:8704 + (ot + 1) * 128],
                colb[0:KV, 24:25], start=True, stop=True, skip_group_check=True),
                waits=[("ve", VE_BKV)] if ot == 0 else [])
        tmark()
        vop(lambda e: e.tensor_copy(pool[:, 0, 0:8], ps_w1[:, 0:8]),
            waits=[("pe", R.c("pe"))])
        vfl()
        vop(lambda e: e.tensor_tensor(
            pool[:, 1, 0:8], pool[:, 0, 0:8], colf[:, 40:48], ALU.add),
            waits=[("dRB", DRB_BVU)])
        vfl()
        vop(lambda e: e.tensor_copy(colb[:, 0:16].rearrange("p (k two) -> p k two", two=2)[:, :, 0:1], pool[:, 1, 0:8].rearrange("p (k one) -> p k one", one=1)))
        vfl()
        VE_U2 = R.c("ve")
        bias_cols(drow["bpr"], C, colf, 40)
        VE_BPR = R.c("ve")
        for ot in range(NKT):
            for kt in range(NKT):
                top(lambda e, ot=ot, kt=kt: e.matmul(
                    ps_w2[:, ot * NKT + kt:ot * NKT + kt + 1],
                    wprojT_v[:, kt, ot * 128:(ot + 1) * 128],
                    colb[:, 2 * kt:2 * kt + 1], start=True, stop=True, skip_group_check=True),
                    waits=[("ve", VE_BPR), ("dW", DW_ALL)] if (ot == 0 and kt == 0) else [])
        tmark()
        vop(lambda e: e.reduce_sum(
            colf2[:, 21:22], ps_w2[:, 0:NKT * NKT]
            .rearrange("p (o k) -> p o k", k=NKT)[:, 0:1, :], axis=AX.X),
            waits=[("pe", R.c("pe"))]) if False else None
        vop(lambda e: e.reduce_sum(
            pool[:, 2, 0:NKT].rearrange("p (o one) -> p o one", one=1),
            ps_w2[:, 0:NKT * NKT].rearrange("p (o k) -> p o k", k=NKT),
            axis=AX.X),
            waits=[("pe", R.c("pe"))])
        vop(lambda e: e.tensor_tensor(colf[:, 8:16], pool[:, 2, 0:NKT],
                                      colf[:, 40:48], ALU.add))
        vfl()
        bias_cols(drow["bqk"], C, colf, 40)
        VE_BQK = R.c("ve")
        for ot in range(NKT):
            for kt in range(4):
                top(lambda e, ot=ot, kt=kt: e.matmul(
                    ps_w1[:, ot * 4 + kt:ot * 4 + kt + 1],
                    wqkT_sb[:, kt, ot * 128:(ot + 1) * 128],
                    colb[:, 16 + 2 * kt:17 + 2 * kt], start=True, stop=True, skip_group_check=True),
                    waits=[("ve", VE_BQK), ("dW", DW_ALL)] if (ot == 0 and kt == 0) else [])
        tmark()
        vop(lambda e: e.reduce_sum(
            pool[:, 2, 8:16].rearrange("p (o one) -> p o one", one=1),
            ps_w1[:, 0:32].rearrange("p (o k) -> p o k", k=4), axis=AX.X),
            waits=[("pe", R.c("pe"))])
        vop(lambda e: e.tensor_tensor(colf[:, 0:8], pool[:, 2, 8:16],
                                      colf[:, 40:48], ALU.add))
        vfl()

        R.op("gpsimd", lambda e: e.memset(pool[:, 3, :], 0.0), inc="gp")
        GP_POOL3 = R.c("gp")
        for kt in range(NKT):
            sop(lambda e, kt=kt: e.activation(
                out=c_sb[:, kt, :], in_=pool[:, 3, :], func=AF.Identity,
                bias=colf2[:, 8 + kt:9 + kt], scale=0.0),
                waits=[("gp", GP_POOL3), ("ve", R.c("ve"))] if kt == 0 else [],
                inc="ac")
        ACC1 = R.c("ac")
        for kt in range(NKT):
            vop(lambda e, kt=kt: e.tensor_copy(cbf_sb[:, 0][:, kt, :], c_sb[:, kt, :]),
                waits=[("ac", ACC1)] if kt == 0 else [])
        vfl()

        if STOP_STAGE == "P0":
            return do_replay()
        for b in range(B):
            for ch in range(8):
                par = ch % 2
                sem = "dXFa" if par == 0 else "dXFb"
                src = dxf[b].rearrange("(k p) t -> p k t", p=128)[:, :, ch * 512:(ch + 1) * 512]
                gdma(ln2_sb[:, par], src, sem,
                     waits=[("pe", R.c("pe")), ("ve", R.c("ve"))])
                dxf_c = R.c(sem)
                bo = par * 1024
                for tt in range(4):
                    vop(lambda e, par=par, tt=tt: e.tensor_tensor(
                        xq_sb[:, 0, :].rearrange("p (k t) -> p k t", k=NKT),
                        ln2_sb[:, par][:, :, tt * 128:(tt + 1) * 128],
                        ln2_sb[:, par][:, :, tt * 128:(tt + 1) * 128], ALU.mult),
                        waits=[(sem, dxf_c), ("pe", R.c("pe"))])
                    velast = vfl()
                    for kt in range(NKT):
                        top(lambda e, par=par, tt=tt, kt=kt, bo=bo: e.matmul(
                            ps_sc[:, bo + tt * 66:bo + tt * 66 + 65],
                            ln2_sb[:, par][:, kt, tt * 128:(tt + 1) * 128],
                            wkv1_sb[:, kt, 0:KV + 1],
                            start=(tt == 0 and kt == 0), stop=False,
                            skip_group_check=True),
                            waits=([(sem, dxf_c), ("ve", velast),
                                    ("ac", R.c("ac"))] if kt == 0 else []))
                    for kt in range(NKT):
                        top(lambda e, par=par, tt=tt, kt=kt, bo=bo: e.matmul(
                            ps_sc[:, bo + tt * 66 + 65:bo + tt * 66 + 66],
                            xq_sb[:, 0, :].rearrange("p (k t) -> p k t", k=NKT)[:, kt, :],
                            ones_col[:, :],
                            start=False, stop=(kt == NKT - 1),
                            skip_group_check=True))
                    tmark()
                pe_g = R.c("pe")
                sums = ps_sc[:, bo:bo + 264].rearrange("p (t c) -> p t c", c=66)[:, :, 64:65]
                sqs = ps_sc[:, bo:bo + 264].rearrange("p (t c) -> p t c", c=66)[:, :, 65:66]
                vop(lambda e, sums=sums: e.tensor_scalar(
                    rcol[:, 0:4].rearrange("p (t o) -> p t o", o=1), sums,
                    1.0 / C, None, ALU.mult), waits=[("pe", pe_g)])
                vop(lambda e, sqs=sqs: e.tensor_scalar(
                    rcol[:, 4:8].rearrange("p (t o) -> p t o", o=1), sqs,
                    1.0 / C, None, ALU.mult))
                vop(lambda e: e.tensor_tensor(pool[:, 7, 0:4], rcol[:, 0:4],
                                              rcol[:, 0:4], ALU.mult))
                vop(lambda e: e.tensor_tensor(pool[:, 7, 4:8], rcol[:, 4:8],
                                              pool[:, 7, 0:4], ALU.subtract))
                vfl()
                sop(lambda e: e.activation(out=pool[:, 7, 8:12], in_=pool[:, 7, 4:8],
                                           func=AF.Ln, bias=eps_col[:, :]),
                    waits=[("ve", R.c("ve"))], inc="ac")
                ac1 = R.c("ac")
                sop(lambda e: e.activation(out=pool[:, 7, 12:16], in_=pool[:, 7, 8:12],
                                           func=AF.Exp, scale=-0.5),
                    waits=[("ac", ac1)], inc="ac")
                ac2 = R.c("ac")
                vop(lambda e: e.tensor_tensor(pool[:, 7, 16:20], rcol[:, 0:4],
                                              pool[:, 7, 12:16], ALU.mult),
                    waits=[("ac", ac2)])
                vfl()
                for tt in range(4):
                    sop(lambda e, tt=tt, bo=bo: e.activation(
                        out=pool[:, 4 + (tt % 2), 0:64],
                        in_=ps_sc[:, bo + tt * 66:bo + tt * 66 + 64],
                        func=AF.Copy, bias=0.0, scale=pool[:, 7, 12 + tt:13 + tt]),
                        waits=[("ve", R.c("ve")), ("pe", R.c("pe"))], inc="ac")
                    acg = R.c("ac")
                    vop(lambda e, tt=tt: e.tensor_scalar(
                        pool[:, 6, 0:64], wbar_b[:, :], pool[:, 7, 16 + tt:17 + tt],
                        None, ALU.mult), waits=[("ac", acg)])
                    vop(lambda e, tt=tt: e.tensor_tensor(
                        pool[:, 6, 64:128], pool[:, 4 + (tt % 2), 0:64],
                        pool[:, 6, 0:64], ALU.subtract))
                    vop(lambda e, tt=tt: e.tensor_copy(
                        poolb[:, 0, tt * 128:tt * 128 + 64], pool[:, 6, 64:128]))
                    vop(lambda e, tt=tt: e.tensor_tensor(
                        pool[:, 6, 128:192], pool[:, 6, 64:128], pool[:, 6, 64:128],
                        ALU.mult))
                    vop(lambda e, b=b, ch=ch, tt=tt: e.reduce_sum(
                        knorm_sb[:, ch * 4 + tt:ch * 4 + tt + 1],
                        pool[:, 6, 128:192], axis=AX.X))
                    vel = vfl()
                    top(lambda e, tt=tt: e.transpose(
                        ps_w1[0:64, tt * 128:(tt + 1) * 128],
                        pool[:, 6, 64:128], ident128[:, :]),
                        waits=[("ve", vel), ("ac", R.c("ac"))])
                tmark()
                pe_t = tfl()
                vop(lambda e: e.tensor_copy(poolb[0:64, 1, :], ps_w1[0:64, :]),
                    waits=[("pe", pe_t)])
                vfl()
                vee = R.c("ve")
                sdma(d_ckv_fm[b, :, ch * 512:(ch + 1) * 512],
                     poolb[0:64, 1, :], "dCS", waits=[("ve", vee)])
                sdma(d_ckv_rm[b, ch * 4:(ch + 1) * 4].rearrange("tt p v -> p tt v"),
                     poolb[:, 0, :].rearrange("p (tt v) -> p tt v", tt=4)[:, :, 0:64],
                     "dCS")
            vop(lambda e: e.reduce_max(rcol[:, 0:1], knorm_sb[:, :], axis=AX.X),
                waits=[("pe", R.c("pe"))])
            vfl()
            top(lambda e: e.transpose(ps_w1[0:1, 256:384], rcol[:, 0:1],
                                      ident128[:, :]),
                waits=[("ve", R.c("ve")), ("ac", R.c("ac"))])
            tmark()
            tfl()
            vop(lambda e: e.tensor_copy(rows2[0:1, 1024:1152], ps_w1[0:1, 256:384]),
                waits=[("pe", R.c("pe"))])
            vop(lambda e, b=b: e.reduce_max(maxk2_sb[:, b:b + 1],
                                            rows2[0:1, 1024:1152], axis=AX.X))
            vfl()
        VE_PRE = R.c("ve")

        if STOP_STAGE == "P1":
            return do_replay()
        gdma(xb_sb[:, :, :], dxo[0].rearrange("(k p) t -> p k t", p=128), "dXB")

        def load_ckv(b):
            sdma(ckvf_sb[0:64, :], d_ckv_fm[b], "dCK",
                 waits=[("pe", R.c("pe")), ("ac", R.c("ac")),
                        ("dCS", R.c("dCS")), ("gp", GP0)])
            sdma(ckvf_sb[64:128, :], d_ckv_fm[b], "dCK")
            sdma(ckvr_sb[:, :, 0:KV],
                 d_ckv_rm[b].rearrange("s p v -> p s v"), "dCK")
            return R.c("dCK")

        load_ckv(0)

        def ln_rows_from(psA, psB, pe_wait):
            vop(lambda e: e.tensor_scalar(rows[0:1, 0:512], psA[0:1, :], 1.0 / C,
                                          None, ALU.mult),
                waits=[("pe", pe_wait)])
            vop(lambda e: e.tensor_scalar(rows[0:1, 512:1024], psB[0:1, :], 1.0 / C,
                                          None, ALU.mult))
            vop(lambda e: e.tensor_tensor(rows[0:1, 1024:1536], rows[0:1, 0:512],
                                          rows[0:1, 0:512], ALU.mult))
            vop(lambda e: e.tensor_tensor(rows[0:1, 512:1024], rows[0:1, 512:1024],
                                          rows[0:1, 1024:1536], ALU.subtract))
            vfl()
            sop(lambda e: e.activation(out=rows[0:1, 1024:1536], in_=rows[0:1, 512:1024],
                                       func=AF.Ln, bias=eps_row[:, :]),
                waits=[("ve", R.c("ve"))], inc="ac")
            a1 = R.c("ac")
            sop(lambda e: e.activation(out=rows2[0:1, 0:512], in_=rows[0:1, 1024:1536],
                                       func=AF.Exp, scale=-0.5),
                waits=[("ac", a1)], inc="ac")
            a2 = R.c("ac")
            vop(lambda e: e.tensor_tensor(rows2[0:1, 512:1024], rows[0:1, 0:512],
                                          rows2[0:1, 0:512], ALU.mult),
                waits=[("ac", a2)])
            vop(lambda e: e.tensor_scalar(rows_bf[:, 512:1024], rows2[0:1, 512:1024],
                                          -1.0, None, ALU.mult))
            vop(lambda e: e.tensor_copy(rows_bf[:, 0:512], rows2[0:1, 0:512]))
            vfl()
            return R.c("ve")

        def ln1_block(b):
            dxb_w = ("dXB", 16 * (b + 1))
            ptm = []
            for kt in range(NKT):
                vop(lambda e, kt=kt: e.tensor_tensor(
                    xq_sb[:, 0, 0:512], xb_sb[:, kt, :], xb_sb[:, kt, :],
                    ALU.mult),
                    waits=([dxb_w, ("pe", R.c("pe"))] if kt == 0 else
                           [("pe", ptm[kt - 1])]))
                vq = vfl()
                top(lambda e, kt=kt: e.matmul(
                    ps_w1[0:1, 0:512], ones_col[:, :], xb_sb[:, kt, :],
                    start=(kt == 0), stop=(kt == NKT - 1), skip_group_check=True),
                    waits=([dxb_w, ("ve", vq), ("ac", R.c("ac"))] if kt == 0
                           else [("ve", vq)]))
                top(lambda e, kt=kt: e.matmul(
                    ps_w2[0:1, 0:512], ones_col[:, :], xq_sb[:, 0, 0:512],
                    start=(kt == 0), stop=(kt == NKT - 1), skip_group_check=True))
                ptm.append(tmark())
            if b == 0 and STOP_STAGE == "s0a1":
                return None
            vR = ln_rows_from(ps_w1, ps_w2, ptm[-1])
            top(lambda e: e.matmul(ps_w1[:, :], ones_row[0:1, :], rows_bf[0:1, 0:512],
                                   start=True, stop=True, skip_group_check=True), waits=[("ve", vR)])
            top(lambda e: e.matmul(ps_w2[:, :], ones_row[0:1, :],
                                   rows_bf[0:1, 512:1024], start=True, stop=True, skip_group_check=True))
            tmark()
            if b == 0 and STOP_STAGE == "s0a2":
                return None
            peB = R.c("pe")
            flc = {}
            for kt in range(NKT):
                sl = kt % 2
                sdma(pool[:, sl, :], dxo[b, kt * 128:(kt + 1) * 128, :], "dXO",
                     waits=[("ve", flc[kt - 2])] if kt >= 2 else
                           [("ve", R.c("ve"))])
                dxoc = R.c("dXO")
                vop(lambda e, sl=sl: e.tensor_tensor(
                    pool[:, 2 + sl, :], pool[:, sl, :], ps_w1[:, :], ALU.mult),
                    waits=[("dXO", dxoc), ("pe", peB)])
                vop(lambda e, kt=kt, sl=sl: e.tensor_tensor(
                    ln1_sb[:, kt, :], pool[:, 2 + sl, :], ps_w2[:, :], ALU.add))
                flc[kt] = vfl()
            return R.c("ve")

        def cq_q_block(b, ve_ln1):
            for ot in range(4):
                bank = ps_ycA if ot % 2 == 0 else ps_ycB
                for kt in range(NKT):
                    top(lambda e, ot=ot, kt=kt, bank=bank: e.matmul(
                        bank[:, :], wqdT_sb[:, kt, ot * 128:(ot + 1) * 128],
                        ln1_sb[:, kt, :], start=(kt == 0), stop=(kt == NKT - 1), skip_group_check=True),
                        waits=[("ve", ve_ln1 if ot == 0 else R.c("ve"))]
                        if kt == 0 else [])
                tmark()
                peq = R.c("pe")
                vop(lambda e, ot=ot, bank=bank: e.tensor_copy(
                    cq_sb[:, ot, :], bank[:, :]), waits=[("pe", peq)])
                vfl()
            for ot in range(NKT):
                bank = ps_ycA if ot % 2 == 0 else ps_ycB
                for kt in range(4):
                    top(lambda e, ot=ot, kt=kt, bank=bank: e.matmul(
                        bank[:, :], wqkT_sb[:, kt, ot * 128:(ot + 1) * 128],
                        cq_sb[:, kt, :], start=(kt == 0), stop=(kt == 3), skip_group_check=True),
                        waits=[("ve", R.c("ve"))] if kt == 0 else [])
                tmark()
                peq = R.c("pe")
                vop(lambda e, ot=ot, bank=bank: e.tensor_scalar(
                    q_sb[:, ot, :], bank[:, :], colf[:, ot:ot + 1], None, ALU.add),
                    waits=[("pe", peq)])
                vfl()
            return R.c("ve")

        ve_nm_by_hp = [0] * NHP

        def negM_block(b, hp):
            vop(lambda e, hp=hp: e.tensor_tensor(
                xq_sb[:, 0, 0:512], q_sb[:, hp, :], q_sb[:, hp, :], ALU.mult),
                waits=[("pe", R.c("pe"))])
            vq = vfl()
            top(lambda e: e.matmul(ps_w2[0:1, 0:512], ones_col[:, :],
                                   xq_sb[:, 0, 0:512], start=True, stop=True, skip_group_check=True),
                waits=[("ve", vq), ("ac", R.c("ac"))])
            tmark()
            pe1 = R.c("pe")
            vop(lambda e: e.reduce_max(tiny[:, 0:1], ps_w2[0:1, 0:512], axis=AX.X),
                waits=[("pe", pe1)])
            vop(lambda e, b=b: e.tensor_tensor(tiny[:, 1:2], tiny[:, 0:1],
                                               maxk2_sb[:, b:b + 1], ALU.mult))
            vfl()
            sop(lambda e: e.activation(out=tiny[:, 2:3], in_=tiny[:, 1:2],
                                       func=AF.Ln, bias=eps_row[:, :]),
                waits=[("ve", R.c("ve"))], inc="ac")
            aa = R.c("ac")
            sop(lambda e: e.activation(out=tiny[:, 3:4], in_=tiny[:, 2:3],
                                       func=AF.Exp, scale=0.5, bias=coef_row[:, :]),
                waits=[("ac", aa)], inc="ac")
            ab = R.c("ac")
            vop(lambda e: e.tensor_scalar(rows_bf[:, 1020:1021], tiny[:, 3:4],
                                          -1.0, None, ALU.mult), waits=[("ac", ab)])
            vq2 = vfl()
            top(lambda e: e.matmul(ps_w2[:, 508:509], ones_row[0:1, :],
                                   rows_bf[0:1, 1020:1021], start=True, stop=True, skip_group_check=True),
                waits=[("ve", vq2)])
            tmark()
            pe2 = R.c("pe")
            vop(lambda e, hp=hp: e.tensor_copy(negM_col[:, hp:hp + 1],
                                               ps_w2[:, 508:509]),
                waits=[("pe", pe2)])
            vfl()
            ve_nm_by_hp[hp] = R.c("ve")

        ve_ln2_done = [0] * B
        DFS_TOT = [0]

        def emit_one_stream(j):
            jp = j % 2
            for mi, nm in enumerate(["wfT", "ufT", "wiT", "uiT", "wcT", "ucT"]):
                dst = strm[:, jp * 6144 + mi * 1024:jp * 6144 + (mi + 1) * 1024] \
                    .rearrange("p (k t) -> p k t", k=NKT)
                src = dW[nm].rearrange("(k p) o -> p k o", p=128)[:, :, j * 128:(j + 1) * 128]
                gdma(dst, src, "dFS",
                     waits=[("pe", R.c("pe"))] if mi == 0 else [])
                DFS_TOT[0] += 16
            DFS_AFTER_J[j] = R.c("dFS")

        DFS_AFTER_J = {}

        def ffn_fillers(b):
            out = []
            step = b - 1
            if step >= 0:
                emit_one_stream(0)
                emit_one_stream(1)
                cold = cbf_sb[:, step % 2]
                cnew = cbf_sb[:, (step + 1) % 2]
                lnp = ln2_sb[:, step % 2]
                for j in range(NKT):
                    jp = j % 2
                    for gi, (wmi, umi, boff, func, gout) in enumerate([
                            (0, 1, 16, AF.Sigmoid, fgate),
                            (2, 3, 24, AF.Sigmoid, igate),
                            (4, 5, 32, AF.Tanh, ggate)]):
                        bank = ps_w1 if gi % 2 == 0 else ps_w2
                        wv = strm[:, jp * 6144 + wmi * 1024:jp * 6144 + (wmi + 1) * 1024].rearrange("p (k t) -> p k t", k=NKT)
                        uv = strm[:, jp * 6144 + umi * 1024:jp * 6144 + (umi + 1) * 1024].rearrange("p (k t) -> p k t", k=NKT)

                        def fgate_all(bank=bank, wv=wv, uv=uv, lnp=lnp,
                                      cold=cold, func=func, gout=gout,
                                      boff=boff, j=j):
                            for kt in range(NKT):
                                top(lambda e, kt=kt: e.matmul(
                                    bank[:, :], wv[:, kt, :], lnp[:, kt, :],
                                    start=(kt == 0), stop=False, skip_group_check=True),
                                    waits=([("dFS", DFS_AFTER_J[j]),
                                            ("ac", R.c("ac")),
                                            ("ve", R.c("ve"))] if kt == 0 else []))
                            for kt in range(NKT):
                                top(lambda e, kt=kt: e.matmul(
                                    bank[:, :], uv[:, kt, :], cold[:, kt, :],
                                    start=False, stop=(kt == NKT - 1), skip_group_check=True))
                            tmark()
                            peg = R.c("pe")
                            sop(lambda e: e.activation(
                                out=gout[:, :], in_=bank[:, :], func=func,
                                bias=colf[:, boff + j:boff + j + 1]),
                                waits=[("pe", peg)], inc="ac")
                        out.append(fgate_all)

                    def fu(j=j, cnew=cnew, step=step):
                        acg = R.c("ac")
                        if TAPS and step == 0 and j == 7:
                            sdma(d_zco[0:128, :], fgate[:, :], "dSP",
                                 waits=[("ac", acg)])
                            sdma(d_zco[128:256, :], igate[:, :], "dSP")
                            sdma(d_zco[256:384, :], ggate[:, :], "dSP")
                        vop(lambda e, j=j: e.tensor_tensor(
                            tfc[:, :], fgate[:, :], c_sb[:, j, :], ALU.mult),
                            waits=[("ac", acg)])
                        vop(lambda e: e.tensor_tensor(
                            tig[:, :], igate[:, :], ggate[:, :], ALU.mult))
                        vop(lambda e, j=j: e.tensor_tensor(
                            c_sb[:, j, :], tfc[:, :], tig[:, :], ALU.add))
                        vop(lambda e, j=j, cnew=cnew: e.tensor_copy(
                            cnew[:, j, :], c_sb[:, j, :]))
                        vfl()
                        if j + 2 < NKT:
                            emit_one_stream(j + 2)
                    out.append(fu)
            merged = []
            gi = 0
            if b < B and not NO_INTERLEAVE:
                for hp in range(1, NHP):
                    merged.append(lambda b=b, hp=hp: negM_block(b, hp))
                    for _ in range(max(1, len(out) // 7)):
                        if gi < len(out):
                            merged.append(out[gi]); gi += 1
            while gi < len(out):
                merged.append(out[gi]); gi += 1
            return merged

        for b in range(B):
            for _i in range(1, NHP):
                ve_nm_by_hp[_i] = 0
            ve_ln1 = ln1_block(b)
            if b == 0 and STOP_STAGE in ("s0a", "s0a1", "s0a2"):
                return do_replay()
            if b + 1 < B:
                gdma(xb_sb[:, :, :], dxo[b + 1].rearrange("(k p) t -> p k t", p=128), "dXB",
                     waits=[("pe", R.c("pe"))])
            ve_q = cq_q_block(b, ve_ln1)
            if b == 0 and STOP_STAGE == "s0b":
                return do_replay()
            negM_block(b, 0)
            if b == 0 and STOP_STAGE == "s0c":
                return do_replay()
            fillers = ffn_fillers(b)
            fi = [0]

            def pop_filler():
                if NO_INTERLEAVE:
                    return
                if fi[0] < len(fillers):
                    fillers[fi[0]]()
                    fi[0] += 1

            dck_need = 48 * (b + 1)
            exp_ids = {}
            yc_pe = {}
            firstA, firstB = [True], [True]
            for hp in range(NHP):
                for g in range(NG):
                    par = g % 2
                    wts = [("ac", R.c("ac"))]
                    if hp == 0 and g == 0:
                        wts += [("dCK", dck_need), ("ve", ve_nm_by_hp[0])]
                    for si in range(2):
                        stile = 2 * g + si
                        top(lambda e, hp=hp, stile=stile, si=si: e.matmul(
                            ps_sc[:, (si * 2) * 512:(si * 2) * 512 + 512],
                            ckvf_sb[0:64, stile * 128:(stile + 1) * 128],
                            q_sb[:, hp, :][0:64, :],
                            start=True, stop=True, tile_position=(0, 0), skip_group_check=True),
                            waits=wts if si == 0 else [])
                        wts = []
                        top(lambda e, hp=hp, stile=stile, si=si: e.matmul(
                            ps_sc[:, (si * 2 + 1) * 512:(si * 2 + 1) * 512 + 512],
                            ckvf_sb[64:128, stile * 128:(stile + 1) * 128],
                            q_sb[:, hp, :][64:128, :],
                            start=True, stop=True, tile_position=(64, 0), skip_group_check=True))
                    sc_pe = tmark()
                    ewaits = [("pe", sc_pe)]
                    if g == 0:
                        assert ve_nm_by_hp[hp] > 0, (b, hp)
                        ewaits.append(("ve", ve_nm_by_hp[hp]))
                    sop(lambda e, hp=hp, par=par: e.activation(
                        out=attn_sb[:, par][:, :], in_=ps_sc[:, :], func=AF.Exp,
                        bias=negM_col[:, hp:hp + 1], scale=0.125),
                        waits=ewaits, inc="ac", drain=False)
                    exp_ids[(hp, g)] = R.c("ac")
                    ycw = [("ac", exp_ids[(hp, g)])]
                    for si in range(2):
                        stile = 2 * g + si
                        for half, bank, fflag in ((0, ps_ycA, firstA),
                                                  (1, ps_ycB, firstB)):
                            for tsub in range(4):
                                stt = fflag[0]
                                fflag[0] = False
                                top(lambda e, par=par, si=si, half=half, tsub=tsub,
                                    stile=stile, bank=bank, stt=stt: e.matmul(
                                        bank[:, tsub * 65:tsub * 65 + 65],
                                        attn_sb[:, par][:, (si * 2 + half) * 512 + tsub * 128:
                                                     (si * 2 + half) * 512 + (tsub + 1) * 128],
                                        ckvr_sb[:, stile, 0:KV + 1],
                                        start=stt, stop=False,
                                        skip_group_check=True),
                                    waits=ycw)
                                ycw = []
                    yc_pe[(hp, g)] = tmark()
                    if b == 0 and hp == 0 and g == 0 and STOP_STAGE == "s0g0":
                        return do_replay()
                    pop_filler()
                pey = yc_pe[(hp, NG - 1)]
                vop(lambda e: e.reciprocal(
                    rcol[:, 0:4].rearrange("p (t o) -> p t o", o=1),
                    ps_ycA[:, 0:260].rearrange("p (t c) -> p t c", c=65)[:, :, 64:65]),
                    waits=[("pe", pey)])
                vop(lambda e: e.reciprocal(
                    rcol[:, 4:8].rearrange("p (t o) -> p t o", o=1),
                    ps_ycB[:, 0:260].rearrange("p (t c) -> p t c", c=65)[:, :, 64:65]))
                vfl()
                vrc = R.c("ve")
                for half, bank in ((0, ps_ycA), (1, ps_ycB)):
                    for tsub in range(4):
                        vop(lambda e, half=half, tsub=tsub, bank=bank: e.tensor_scalar(
                            pool[:, 4 + half, tsub * 128:tsub * 128 + 64],
                            bank[:, tsub * 65:tsub * 65 + 64],
                            rcol[:, half * 4 + tsub:half * 4 + tsub + 1], None,
                            ALU.mult),
                            waits=([("ve", vrc), ("pe", R.c("pe"))]
                                   if (half == 0 and tsub == 0) else []))
                vnm = vfl()
                for half in range(2):
                    for tsub in range(4):
                        top(lambda e, half=half, tsub=tsub: e.transpose(
                            (ps_w1 if half == 0 else ps_w2)[0:64, tsub * 128:(tsub + 1) * 128],
                            pool[:, 4 + half, tsub * 128:tsub * 128 + 64],
                            ident128[:, :]),
                            waits=([("ve", vnm), ("ac", R.c("ac"))]
                                   if (half == 0 and tsub == 0) else []))
                tmark()
                pet = tfl()
                vop(lambda e, hp=hp: e.tensor_copy(ycn_sb[0:64, hp, :],
                                                   ps_w1[0:64, :]),
                    waits=[("pe", pet)])
                vop(lambda e, hp=hp: e.tensor_copy(ycn_sb[64:128, hp, :],
                                                   ps_w2[0:64, :]))
                vfl()
                firstA[0] = True
                firstB[0] = True
                if b == 0 and hp == 0 and STOP_STAGE == "s0hp0":
                    return do_replay()
                pop_filler()
            VE_YCN = R.c("ve")
            if b == 0 and STOP_STAGE == "s0inner":
                return do_replay()
            while fi[0] < len(fillers):
                pop_filler()

            ptm = {}
            for ot in range(NKT):
                bank = ps_w1 if ot % 2 == 0 else ps_w2
                for kt in range(NKT):
                    top(lambda e, ot=ot, kt=kt, bank=bank: e.matmul(
                        bank[:, :], WcombT_sb[:, kt, ot * 128:(ot + 1) * 128],
                        ycn_sb[:, kt, :], start=(kt == 0), stop=(kt == NKT - 1), skip_group_check=True),
                        waits=([("ve", R.c("ve")), ("ac", R.c("ac"))]
                               if kt == 0 else []))
                peP = tmark()
                sl = ot % 2
                sop(lambda e, bank=bank, sl=sl, ot=ot: e.activation(
                    out=pool[:, 2 + sl, :], in_=bank[:, :], func=AF.Identity,
                    bias=colf[:, 8 + ot:9 + ot]),
                    waits=[("pe", peP)], inc="ac")
                acP = R.c("ac")
                sdma(pool[:, sl, :], dxo[b, ot * 128:(ot + 1) * 128, :], "dXO",
                     waits=[("ve", R.c("ve"))])
                dxoc = R.c("dXO")
                vop(lambda e, sl=sl: e.tensor_tensor(
                    pool[:, 6, :], pool[:, sl, :], pool[:, 2 + sl, :], ALU.add),
                    waits=[("ac", acP), ("dXO", dxoc), ("dSP", R.c("dSP")),
                           ("pe", ptm.get(ot - 1, 0))])
                vop(lambda e, sl=sl: e.tensor_copy(poolb[:, sl, :], pool[:, 6, :]))
                vop(lambda e, sl=sl: e.tensor_tensor(
                    xq_sb[:, 0, 0:512], poolb[:, sl, :], poolb[:, sl, :],
                    ALU.mult))
                vq = vfl()
                top(lambda e, ot=ot, sl=sl: e.matmul(
                    ps_ycA[0:1, :], ones_col[:, :], poolb[:, sl, :],
                    start=(ot == 0), stop=(ot == NKT - 1), skip_group_check=True),
                    waits=[("ve", vq)])
                top(lambda e, ot=ot, sl=sl: e.matmul(
                    ps_ycB[0:1, :], ones_col[:, :], xq_sb[:, 0, 0:512],
                    start=(ot == 0), stop=(ot == NKT - 1), skip_group_check=True))
                ptm[ot] = tmark()
                sdma(d_x1[b, ot * 128:(ot + 1) * 128, :], pool[:, 6, :], "dSP",
                     waits=[("ve", vq)])
            if b == 0 and STOP_STAGE == "s0proj":
                return do_replay()
            vR2 = ln_rows_from(ps_ycA, ps_ycB, ptm[NKT - 1])
            top(lambda e: e.matmul(ps_ycA[:, :], ones_row[0:1, :],
                                   rows_bf[0:1, 0:512], start=True, stop=True, skip_group_check=True),
                waits=[("ve", vR2)])
            top(lambda e: e.matmul(ps_ycB[:, :], ones_row[0:1, :],
                                   rows_bf[0:1, 512:1024], start=True, stop=True, skip_group_check=True))
            tmark()
            peB2 = R.c("pe")
            lnc = ln2_sb[:, b % 2]
            dsp_need = R.c("dSP")
            flc = {}
            for kt in range(NKT):
                sl = kt % 2
                sdma(pool[:, sl, :], d_x1[b, kt * 128:(kt + 1) * 128, :], "dRB",
                     waits=([("dSP", dsp_need)] if kt < 2 else
                            [("ve", flc[kt - 2])]))
                drbc = R.c("dRB")
                vop(lambda e, sl=sl: e.tensor_tensor(
                    pool[:, 2 + sl, :], pool[:, sl, :], ps_ycA[:, :], ALU.mult),
                    waits=[("dRB", drbc), ("pe", peB2)])
                vop(lambda e, kt=kt, sl=sl, lnc=lnc: e.tensor_tensor(
                    lnc[:, kt, :], pool[:, 2 + sl, :], ps_ycB[:, :], ALU.add))
                flc[kt] = vfl()
            ve_ln2_done[b] = R.c("ve")
            sdma(d_ln2[b].rearrange("(k p) t -> p k t", p=128), lnc, "dSP",
                 waits=[("ve", ve_ln2_done[b])])
            if b + 1 < B:
                load_ckv(b + 1)
            if STOP_STAGE == f"slot{b}":
                return do_replay()

        for f in ffn_fillers(B):
            f()
        if STOP_STAGE == "ffn7":
            return do_replay()

        cfin = cbf_sb[:, B % 2]
        uoT_v = strm[:, 0:8192].rearrange("p (k t) -> p k t", k=NKT)
        gdma(uoT_v, dW["uoT"].rearrange("(k p) o -> p k o", p=128), "dFS",
             waits=[("pe", R.c("pe"))])
        DFS_TOT[0] += 16
        dfs_uo = R.c("dFS")
        for ot in range(NKT):
            bank = ps_w1 if ot % 2 == 0 else ps_w2
            for kt in range(NKT):
                top(lambda e, ot=ot, kt=kt, bank=bank: e.matmul(
                    bank[:, :], uoT_v[:, kt, ot * 128:(ot + 1) * 128],
                    cfin[:, kt, :], start=(kt == 0), stop=(kt == NKT - 1), skip_group_check=True),
                    waits=([("dFS", dfs_uo), ("ve", R.c("ve")), ("ac", R.c("ac"))]
                           if kt == 0 else []))
            tmark()
            pez = R.c("pe")
            sl = ot % 2
            vop(lambda e, bank=bank, sl=sl: e.tensor_copy(pool[:, sl, :],
                                                          bank[:, :]),
                waits=[("pe", pez), ("dSP", R.c("dSP"))])
            vq = vfl()
            sdma(d_zco[ot * 128:(ot + 1) * 128, :], pool[:, sl, :], "dSP",
                 waits=[("ve", vq)])
        woT_v = strm[:, 0:8192].rearrange("p (k t) -> p k t", k=NKT)
        gdma(woT_v, dW["woT"].rearrange("(k p) o -> p k o", p=128), "dFS",
             waits=[("pe", R.c("pe"))])
        DFS_TOT[0] += 16
        dfs_wo = R.c("dFS")
        dsp_zco = R.c("dSP")

        drb_ln_by_b = {}
        sdma(ln2_sb[:, 0], d_ln2[0].rearrange("(k p) t -> p k t", p=128), "dRB",
             waits=[("pe", R.c("pe")), ("ac", R.c("ac")), ("ve", R.c("ve")),
                    ("dSP", R.c("dSP"))])
        drb_ln_by_b[0] = R.c("dRB")
        for b in range(B):
            if b + 1 < B:
                sdma(ln2_sb[:, (b + 1) % 2],
                     d_ln2[b + 1].rearrange("(k p) t -> p k t", p=128), "dRB",
                     waits=[("pe", R.c("pe"))])
                drb_ln_by_b[b + 1] = R.c("dRB")
            drb_ln = drb_ln_by_b[b]
            lnslot = ln2_sb[:, b % 2]
            for ot in range(NKT):
                bank = ps_w1 if ot % 2 == 0 else ps_w2
                for kt in range(NKT):
                    top(lambda e, ot=ot, kt=kt, bank=bank, lnslot=lnslot: e.matmul(
                        bank[:, :], woT_v[:, kt, ot * 128:(ot + 1) * 128],
                        lnslot[:, kt, :], start=(kt == 0),
                        stop=(kt == NKT - 1), skip_group_check=True),
                        waits=([("dFS", dfs_wo), ("dRB", drb_ln),
                                ("ve", R.c("ve")), ("ac", R.c("ac"))]
                               if (ot == 0 and kt == 0) else
                               ([("dRB", drb_ln), ("ve", R.c("ve"))] if kt == 0 else [])))
                peh = tmark()
                sl = ot % 2
                sdma(pool[:, 4 + sl, :], d_zco[ot * 128:(ot + 1) * 128, :], "dZC",
                     waits=([("dSP", dsp_zco)] if (b == 0 and ot < 2) else
                            [("ve", R.c("ve"))]))
                dzc = R.c("dZC")
                sdma(pool[:, sl, :], d_x1[b, ot * 128:(ot + 1) * 128, :], "dRB",
                     waits=[("ve", R.c("ve"))])
                drbx = R.c("dRB")
                vop(lambda e, sl=sl, bank=bank: e.tensor_tensor(
                    pool[:, 6, :], bank[:, :], pool[:, 4 + sl, :], ALU.add),
                    waits=[("pe", peh), ("dZC", dzc), ("ac", R.c("ac"))])
                vh = vfl()
                sop(lambda e, ot=ot: e.activation(
                    out=pool[:, 7, :], in_=pool[:, 6, :], func=AF.Tanh,
                    bias=colf2[:, ot:ot + 1]),
                    waits=[("ve", vh)], inc="ac")
                ach = R.c("ac")
                vop(lambda e, sl=sl: e.tensor_tensor(
                    pool[:, 2 + sl, :], pool[:, sl, :], pool[:, 7, :], ALU.add),
                    waits=[("ac", ach), ("dRB", drbx), ("dOU", R.c("dOU"))])
                vo = vfl()
                sdma(dy[b, ot * 128:(ot + 1) * 128, :], pool[:, 2 + sl, :], "dOU",
                     waits=[("ve", vo)])
        return do_replay()



def kernel(**inputs):
    xs = np.ascontiguousarray(np.asarray(inputs["L0_attn_x"], np.float32)
                              .transpose(0, 2, 1))
    host = {
        "xf": xs,
        "wqdT": np.ascontiguousarray(np.asarray(inputs["L0_attn_wq_d"], np.float32).T),
        "wqkT": np.ascontiguousarray(np.asarray(inputs["L0_attn_w_qk"], np.float32).T),
        "wkvT": np.ascontiguousarray(np.asarray(inputs["L0_attn_wkv_d"], np.float32).T),
        "wvuT": np.ascontiguousarray(np.asarray(inputs["L0_attn_wv_u"], np.float32).T),
        "wvu": np.ascontiguousarray(np.asarray(inputs["L0_attn_wv_u"], np.float32)),
        "wprT": np.ascontiguousarray(np.asarray(inputs["L0_attn_w_proj"], np.float32).T),
        "bqd": np.asarray(inputs["L0_attn_bq_d"], np.float32),
        "bqk": np.asarray(inputs["L0_attn_b_qk"], np.float32),
        "bkv": np.asarray(inputs["L0_attn_bkv_d"], np.float32),
        "bvu": np.asarray(inputs["L0_attn_bv_u"], np.float32),
        "bpr": np.asarray(inputs["L0_attn_b_proj"], np.float32),
        "wfT": np.ascontiguousarray(np.asarray(inputs["L0_ffn_wf"], np.float32).T),
        "ufT": np.ascontiguousarray(np.asarray(inputs["L0_ffn_uf"], np.float32).T),
        "wiT": np.ascontiguousarray(np.asarray(inputs["L0_ffn_wi"], np.float32).T),
        "uiT": np.ascontiguousarray(np.asarray(inputs["L0_ffn_ui"], np.float32).T),
        "wcT": np.ascontiguousarray(np.asarray(inputs["L0_ffn_wc"], np.float32).T),
        "ucT": np.ascontiguousarray(np.asarray(inputs["L0_ffn_uc"], np.float32).T),
        "woT": np.ascontiguousarray(np.asarray(inputs["L0_ffn_wo"], np.float32).T),
        "uoT": np.ascontiguousarray(np.asarray(inputs["L0_ffn_uo"], np.float32).T),
        "bfb": np.asarray(inputs["L0_ffn_bf"], np.float32),
        "bib": np.asarray(inputs["L0_ffn_bi"], np.float32),
        "bcb": np.asarray(inputs["L0_ffn_bc"], np.float32),
        "bob": np.asarray(inputs["L0_ffn_bo"], np.float32),
        "incb": np.asarray(inputs["L0_ffn_in_c"], np.float32),
    }
    nc = build_program()
    in_maps = []
    for k in range(NCORES):
        m = dict(host)
        m["xo"] = np.ascontiguousarray(xs[:, :, k * TSH:(k + 1) * TSH])
        in_maps.append(m)
    res = run_bass_kernel_spmd(nc, in_maps, core_ids=list(range(NCORES)))
    shards = [res.results[k]["y"] for k in range(NCORES)]
    full = np.concatenate(shards, axis=2)
    return np.ascontiguousarray(full.transpose(0, 2, 1)).astype(np.float32)
